# version 10
# baseline (speedup 1.0000x reference)
import math
import numpy as np
import ml_dtypes
import concourse.bass as bass
import concourse.mybir as mybir
from concourse.bass_utils import run_bass_kernel_spmd

F32 = mybir.dt.float32
BF16 = mybir.dt.bfloat16
AF = mybir.ActivationFunctionType
ALU = mybir.AluOpType

B, SQ, D = 2, 4096, 2048
NH, HN = 16, 128
H = 8192
NCORE = 8
TOK = 1024
NORM = 1.0 / math.sqrt(HN)
FWHT_SCALE = 1.0 / math.sqrt(H)
LN_EPS = 1e-5
X = mybir.AxisListType.X


class Sem:
    def __init__(self, ctx, nc, name):
        self.h = ctx.enter_context(nc.semaphore(name))
        self.n = 0

    def inc(self, instr, k=1):
        instr.then_inc(self.h, k)
        self.n += k
        return self.n


class Chain:
    def __init__(self, eng, chain_sem):
        self.e = eng
        self.cs = chain_sem
        self.last = None

    def w(self):
        if self.last is not None:
            self.e.wait_ge(*self.last)

    def op(self, instr, sem=None, k=1):
        s = sem if sem is not None else self.cs
        n = s.inc(instr, k)
        self.last = (s.h, n)
        return instr


def _hadamard(n):
    h = np.array([[1.0]], dtype=np.float32)
    while h.shape[0] < n:
        h = np.block([[h, h], [h, -h]])
    return h


def build_program():
    from contextlib import ExitStack
    nc = bass.Bass("TRN2", target_bir_lowering=False,
                   detect_race_conditions=False)
    ctx = ExitStack()

    x_hbm = nc.dram_tensor("x_b", [SQ, D], F32, kind="ExternalInput").ap()
    xresT_hbm = nc.dram_tensor("x_resT", [D, TOK], F32, kind="ExternalInput").ap()
    wqk_hbm = nc.dram_tensor("wqk", [8, 128, 2048], F32, kind="ExternalInput").ap()
    wv_hbm = nc.dram_tensor("wv", [128, 16 * 512], F32, kind="ExternalInput").ap()
    wp_hbm = nc.dram_tensor("wp", [16, 128, 2048], F32, kind="ExternalInput").ap()
    bprojT_hbm = nc.dram_tensor("bprojT", [128, 16], F32, kind="ExternalInput").ap()
    w1_hbm = nc.dram_tensor("w1", [64, 128, 2048], F32, kind="ExternalInput").ap()
    w3_hbm = nc.dram_tensor("w3", [64, 128, 2048], F32, kind="ExternalInput").ap()
    w2_hbm = nc.dram_tensor("w2", [16, 128, 8192], F32, kind="ExternalInput").ap()
    h128_hbm = nc.dram_tensor("h128", [128, 128], BF16, kind="ExternalInput").ap()
    outT_hbm = nc.dram_tensor("outT", [D, TOK], F32, kind="ExternalOutput").ap()

    ag_in = nc.dram_tensor("ag_in", [512, SQ], BF16)
    ag_out = nc.dram_tensor("ag_out", [4096, SQ], BF16, addr_space="Shared")

    sb = ctx.enter_context
    h128_sb = sb(nc.sbuf_tensor("h128_sb", [128, 128], BF16))
    nh128_sb = sb(nc.sbuf_tensor("nh128_sb", [128, 128], BF16))
    ones_mat = sb(nc.sbuf_tensor("ones_mat", [128, 128], BF16))
    bprojT_sb = sb(nc.sbuf_tensor("bprojT_sb", [128, 16], F32))
    negC = sb(nc.sbuf_tensor("negC", [128, 4], F32))
    eps_sb = sb(nc.sbuf_tensor("eps_sb", [128, 1], F32))
    scQ = ExitStack()
    qkT = [scQ.enter_context(nc.sbuf_tensor(f"qkT{i}", [128, SQ], BF16))
           for i in range(8)]
    v_nat = scQ.enter_context(nc.sbuf_tensor("v_nat", [128, 32 * 512], BF16))
    scW = ExitStack()
    wqk_sb = scW.enter_context(nc.sbuf_tensor("wqk_sb", [128, 8 * 2048], BF16))
    wv_sb = scW.enter_context(nc.sbuf_tensor("wv_sb", [128, 16 * 512], BF16))

    g_x = [Sem(ctx, nc, "g_x0"), Sem(ctx, nc, "g_x1")]
    g_w = Sem(ctx, nc, "g_w")
    s_wc = Sem(ctx, nc, "s_wc")
    v_st = Sem(ctx, nc, "v_st")
    s_sq = Sem(ctx, nc, "s_sq")
    v_ln = Sem(ctx, nc, "v_ln")
    s_tr = [Sem(ctx, nc, "s_tr0"), Sem(ctx, nc, "s_tr1")]
    t_qk = Sem(ctx, nc, "t_qk")
    s_qev = Sem(ctx, nc, "s_qev")
    t_v = Sem(ctx, nc, "t_v")
    s_vev = Sem(ctx, nc, "s_vev")
    s_csq = Sem(ctx, nc, "s_csq")
    t_cg = Sem(ctx, nc, "t_cg")
    v_cg = Sem(ctx, nc, "v_cg")
    s_cg = Sem(ctx, nc, "s_cg")
    v_cgf = Sem(ctx, nc, "v_cgf")
    t_sc = Sem(ctx, nc, "t_sc")
    s_exp = Sem(ctx, nc, "s_exp")
    t_ctx = Sem(ctx, nc, "t_ctx")
    v_cx = Sem(ctx, nc, "v_cx")
    c_agi = Sem(ctx, nc, "c_agi")
    s_rs2 = Sem(ctx, nc, "s_rs2")
    c_ctxg = Sem(ctx, nc, "c_ctxg")
    c_rs = Sem(ctx, nc, "c_rs")
    g_wp = [Sem(ctx, nc, f"g_wp{i}") for i in range(2)]
    g_xr = Sem(ctx, nc, "g_xr")
    t_st = Sem(ctx, nc, "t_st")
    v_ffs = Sem(ctx, nc, "v_ffs")
    s_ffq = Sem(ctx, nc, "s_ffq")
    v_xf = Sem(ctx, nc, "v_xf")
    g_w13 = [Sem(ctx, nc, f"g_w13{i}") for i in range(4)]
    t_ff = Sem(ctx, nc, "t_ff")
    s_si = Sem(ctx, nc, "s_si")
    v_hd = Sem(ctx, nc, "v_hd")
    v_fw = Sem(ctx, nc, "v_fw")
    g_fw = Sem(ctx, nc, "g_fw")
    gch = Sem(ctx, nc, "gch")
    t_h = Sem(ctx, nc, "t_h")
    s_h = Sem(ctx, nc, "s_h")
    g_w2 = [Sem(ctx, nc, f"g_w2{i}") for i in range(3)]
    t_w2 = Sem(ctx, nc, "t_w2")
    v_out = Sem(ctx, nc, "v_out")
    vch = Sem(ctx, nc, "vch")
    s_out = [Sem(ctx, nc, "s_out0"), Sem(ctx, nc, "s_out1")]

    with nc.Block("ph0") as blk:
        @blk.gpsimd
        def _(g):
            g.memset(ones_mat[:, :], 1.0)
            g.memset(eps_sb[:, :], LN_EPS)
            for ct in range(8):
                g_w.inc(g.dma_start(out=wqk_sb[:, ct * 2048:(ct + 1) * 2048],
                                    in_=wqk_hbm[ct, :, :]), 16)
            g_w.inc(g.dma_start(out=wv_sb[:, :], in_=wv_hbm), 16)

        @blk.sync
        def _(sy):
            s_wc.inc(sy.dma_start(out=h128_sb[:, :], in_=h128_hbm), 16)
            s_wc.inc(sy.dma_start(out=bprojT_sb[:, :], in_=bprojT_hbm), 16)

        @blk.vector
        def _(v):
            v.wait_ge(s_wc.h, 32)
            v.tensor_scalar_mul(nh128_sb[:, :], h128_sb[:, :], -1.0)

    phA = ExitStack()
    XT = [phA.enter_context(nc.sbuf_tensor(f"XT{i}", [128, 16 * 512], BF16))
          for i in range(2)]
    x_nat = [phA.enter_context(nc.sbuf_tensor(f"x_nat{i}", [128, D], BF16))
             for i in range(2)]
    xn_t = [phA.enter_context(nc.sbuf_tensor(f"xn{i}", [128, D], BF16))
            for i in range(2)]
    stats = phA.enter_context(nc.sbuf_tensor("stats", [128, 64], F32))
    ps_qk = phA.enter_context(nc.psum_tensor("ps_qk", [128, 2 * 512], F32))
    ps_qk2 = phA.enter_context(nc.psum_tensor("ps_qk2", [128, 2 * 512], F32))
    ps_v = phA.enter_context(nc.psum_tensor("ps_v", [128, 2 * 512], F32))

    NT, NBLK = 32, 8

    def st(par, col):
        return stats[:, par * 32 + col:par * 32 + col + 1]

    with nc.Block("phA") as blk:
        @blk.gpsimd
        def _(g):
            for gt in range(NT):
                if gt >= 2:
                    g.wait_ge(v_ln.h, gt - 1)
                g_x[gt % 2].inc(g.dma_start(
                    out=x_nat[gt % 2][:, :],
                    in_=x_hbm[gt * 128:(gt + 1) * 128, :]), 16)

        @blk.vector
        def _(v):
            C = Chain(v, vch)
            for gt in range(NT):
                par = gt % 2
                v.wait_ge(g_x[gt % 2].h, 16 * (gt // 2 + 1))
                if gt >= 2:
                    v.wait_ge(s_tr[gt % 2].h, 16 * (gt // 2))
                xv, xn = x_nat[par], xn_t[par]
                b0 = par * 32
                C.w()
                for q4 in range(4):
                    ins = v.bn_stats(stats[:, b0 + q4 * 6:b0 + (q4 + 1) * 6],
                                     xv[:, q4 * 512:(q4 + 1) * 512])
                C.op(ins)
                C.w(); C.op(v.bn_aggr(stats[:, b0 + 24:b0 + 26],
                                      stats[:, b0:b0 + 24]), v_st)
                v.wait_ge(s_sq.h, gt + 1)
                C.w(); C.op(v.reciprocal(st(par, 27), st(par, 26)))
                C.w(); C.op(v.tensor_scalar(xn[:, :], xv[:, :], st(par, 24),
                                            st(par, 27), ALU.subtract, ALU.mult),
                            v_ln)

        @blk.sync
        def _(sy):
            for gt in range(NT):
                bi, t_loc = gt // 4, gt % 4
                if t_loc == 0 and bi >= 2:
                    sy.wait_ge(t_v.h, 4 * (bi - 1))
                sy.wait_ge(v_ln.h, gt + 1)
                xtv = XT[bi % 2][:, :].rearrange("p (j c) -> p j c", j=16)
                s_tr[gt % 2].inc(sy.dma_start_transpose(
                    xtv[:, :, t_loc * 128:(t_loc + 1) * 128],
                    xn_t[gt % 2][:, :].rearrange("p (j c) -> p j c", j=16)), 16)

        @blk.tensor
        def _(t):
            t.wait_ge(g_w.h, g_w.n)
            for bi in range(NBLK):
                xt = XT[bi % 2]
                t.wait_ge(s_tr[0].h, 16 * 2 * (bi + 1))
                t.wait_ge(s_tr[1].h, 16 * 2 * (bi + 1))
                for ct in range(8):
                    ps = ps_qk if ct % 2 == 0 else ps_qk2
                    idx = bi * 8 + ct
                    if idx >= 2:
                        t.wait_ge(s_qev.h, idx - 1)
                    for d in range(16):
                        mm = t.matmul(ps[:, 0:512],
                                      wqk_sb[:, ct * 2048 + d * 128:
                                             ct * 2048 + (d + 1) * 128],
                                      xt[:, d * 512:(d + 1) * 512],
                                      start=(d == 0), stop=(d == 15))
                    t_qk.inc(mm)
                for i in range(4):
                    half = i % 2
                    idx = bi * 4 + i
                    if idx >= 2:
                        t.wait_ge(s_vev.h, idx - 1)
                    for d in range(16):
                        mm = t.matmul(ps_v[:, half * 512:(half + 1) * 512],
                                      xt[:, d * 512 + i * 128:
                                         d * 512 + (i + 1) * 128],
                                      wv_sb[:, d * 512:(d + 1) * 512],
                                      start=(d == 0), stop=(d == 15))
                    t_v.inc(mm)

        @blk.scalar
        def _(s):
            for bi in range(NBLK + 1):
                if bi < NBLK:
                    for t_loc in range(4):
                        gt = bi * 4 + t_loc
                        par = gt % 2
                        s.wait_ge(v_st.h, gt + 1)
                        s_sq.inc(s.activation(st(par, 26), st(par, 25),
                                              AF.Sqrt, bias=eps_sb[:, 0:1]))
                if bi >= 1:
                    bj = bi - 1
                    for ct in range(8):
                        ps = ps_qk if ct % 2 == 0 else ps_qk2
                        s.wait_ge(t_qk.h, bj * 8 + ct + 1)
                        s_qev.inc(s.copy(
                            qkT[ct][:, bj * 512:(bj + 1) * 512], ps[:, 0:512]))
                    for i in range(4):
                        half = i % 2
                        s.wait_ge(t_v.h, bj * 4 + i + 1)
                        s_vev.inc(s.copy(
                            v_nat[:, (bj * 4 + i) * 512:(bj * 4 + i + 1) * 512],
                            ps_v[:, half * 512:(half + 1) * 512]))

    phA.close()
    scW.close()

    scXF = ExitStack()
    xt_ffn = scXF.enter_context(nc.sbuf_tensor("xt_ffn", [128, 16 * 1024],
                                               BF16, side="right"))
    scC = ExitStack()
    ctxT = [scC.enter_context(nc.sbuf_tensor(f"ctxT{g}", [128, SQ], BF16,
                                             side="right"))
            for g in range(4)]
    phB = ExitStack()
    sq_sc = phB.enter_context(nc.sbuf_tensor("sq_sc", [128, SQ], BF16))
    cg_tmp = phB.enter_context(nc.sbuf_tensor("cg_tmp", [128, 4], F32))
    E = [phB.enter_context(nc.sbuf_tensor(f"E{i}", [128, 512], BF16))
         for i in range(3)]
    rn_rec = phB.enter_context(nc.sbuf_tensor("rn_rec", [128, 1024], F32))
    sqt = [phB.enter_context(nc.sbuf_tensor(f"sqt{i}", [128, 1024], BF16))
           for i in range(2)]
    stt_f = phB.enter_context(nc.sbuf_tensor("stt_f", [128, 3 * 1024], F32))
    bc_b = phB.enter_context(nc.sbuf_tensor("bc_b", [128, 2 * 1024], F32))
    ps_all = phB.enter_context(nc.psum_tensor("ps_all", [128, 4096], F32))

    with nc.Block("phB_cg") as blk:
        @blk.scalar
        def _(s):
            for g in range(4):
                for which in range(2):
                    n = g * 2 + which
                    src = qkT[g] if which == 0 else qkT[4 + g]
                    if n >= 1:
                        s.wait_ge(t_cg.h, 8 * n)
                    s_csq.inc(s.activation(sq_sc[:, :], src[:, :], AF.Square))
                s.wait_ge(v_cg.h, (g + 1) * 16)
                s_cg.inc(s.activation(cg_tmp[:, 2:3], cg_tmp[:, 0:1], AF.Sqrt))
                s_cg.inc(s.activation(cg_tmp[:, 3:4], cg_tmp[:, 1:2], AF.Sqrt))

        @blk.tensor
        def _(t):
            cnt = 0
            for g in range(4):
                for which in range(2):
                    t.wait_ge(s_csq.h, g * 2 + which + 1)
                    for ch in range(8):
                        cnt += 1
                        if cnt >= 3:
                            t.wait_ge(v_cg.h, cnt - 2)
                        t_cg.inc(t.matmul(
                            ps_all[:, (cnt % 2) * 512:(cnt % 2) * 512 + 512],
                            ones_mat[:, :],
                            sq_sc[:, ch * 512:(ch + 1) * 512],
                            start=True, stop=True))

        @blk.vector
        def _(v):
            C = Chain(v, vch)
            cnt = 0
            for g in range(4):
                for which in range(2):
                    col = cg_tmp[:, which:which + 1]
                    for ch in range(8):
                        cnt += 1
                        v.wait_ge(t_cg.h, cnt)
                        pb = ps_all[:, (cnt % 2) * 512:(cnt % 2) * 512 + 512]
                        if ch == 0:
                            C.w(); C.op(v.reduce_max(col, pb, axis=X), v_cg)
                        else:
                            C.w(); C.op(v.reduce_max(rn_rec[:, 0:1], pb, axis=X))
                            C.w(); C.op(v.tensor_max(col, col, rn_rec[:, 0:1]),
                                        v_cg)
                v.wait_ge(s_cg.h, (g + 1) * 2)
                C.w(); C.op(v.tensor_mul(negC[:, g:g + 1], cg_tmp[:, 2:3],
                                         cg_tmp[:, 3:4]))
                C.w(); C.op(v.tensor_scalar_mul(negC[:, g:g + 1],
                                                negC[:, g:g + 1], -NORM), v_cgf)

    with nc.Block("phB") as blk:
        NQB = 8
        @blk.tensor
        def _(t):
            t.wait_ge(v_cgf.h, 4)
            steps = [(g, qb, kt) for g in range(4) for qb in range(NQB)
                     for kt in range(32)]
            nsteps = len(steps)

            def scores(n):
                g, qb, kt = steps[n]
                sc = ps_all[:, (n % 3) * 512:(n % 3) * 512 + 512]
                if n >= 3:
                    t.wait_ge(s_exp.h, n - 2)
                t_sc.inc(t.matmul(sc, qkT[4 + g][:, kt * 128:(kt + 1) * 128],
                                  qkT[g][:, qb * 512:(qb + 1) * 512],
                                  start=True, stop=True))

            scores(0)
            for n in range(nsteps):
                g, qb, kt = steps[n]
                qf = g * NQB + qb
                s0 = 2048 + (qf % 2) * 1024
                if n + 1 < nsteps:
                    scores(n + 1)
                t.wait_ge(s_exp.h, n + 1)
                if kt == 0 and qf >= 2:
                    t.wait_ge(v_cx.h, qf - 1)
                eb = E[n % 3]
                vsl = v_nat[:, kt * 512 + g * 128:kt * 512 + (g + 1) * 128]
                t.matmul(ps_all[:, s0:s0 + 512], vsl, eb[:, :],
                         start=(kt == 0), stop=(kt == 31))
                mm = t.matmul(ps_all[:, s0 + 512:s0 + 1024], ones_mat[:, :],
                              eb[:, :], start=(kt == 0), stop=(kt == 31))
                if kt == 31:
                    t_ctx.inc(mm)
            t.wait_ge(s_exp.h, nsteps)
            t.wait_ge(g_xr.h, 16)
            for d in range(16):
                t.wait_ge(s_ffq.h, d + 1)
                for th in range(2):
                    t.matmul(ps_all[:, th * 512:(th + 1) * 512], ones_mat[:, :],
                             xt_ffn[:, d * 1024 + th * 512:
                                    d * 1024 + (th + 1) * 512],
                             start=(d == 0), stop=(d == 15))
                    mm = t.matmul(ps_all[:, 1024 + th * 512:1024 + (th + 1) * 512],
                                  ones_mat[:, :],
                                  sqt[d % 2][:, th * 512:(th + 1) * 512],
                                  start=(d == 0), stop=(d == 15))
                t_st.inc(mm)

        @blk.scalar
        def _(s):
            s.wait_ge(v_cgf.h, 4)
            n = 0
            for g in range(4):
                for qb in range(NQB):
                    for kt in range(32):
                        n += 1
                        s.wait_ge(t_sc.h, n)
                        sc = ps_all[:, ((n - 1) % 3) * 512:((n - 1) % 3) * 512 + 512]
                        s_exp.inc(s.activation(E[(n - 1) % 3][:, :], sc,
                                               AF.Exp, bias=negC[:, g:g + 1],
                                               scale=NORM))
            s.wait_ge(g_xr.h, 16)
            for d in range(16):
                if d >= 2:
                    s.wait_ge(t_st.h, d - 1)
                s_ffq.inc(s.activation(sqt[d % 2][:, :],
                                       xt_ffn[:, d * 1024:(d + 1) * 1024],
                                       AF.Square))
            s.wait_ge(v_ffs.h, 1)
            s_ffq.inc(s.activation(stt_f[:, 1024:2048], stt_f[:, 1024:2048],
                                   AF.Sqrt))

        @blk.vector
        def _(v):
            C = Chain(v, vch)
            for qf in range(4 * NQB):
                g, qb = qf // NQB, qf % NQB
                s0 = 2048 + (qf % 2) * 1024
                v.wait_ge(t_ctx.h, qf + 1)
                C.w(); C.op(v.reciprocal(rn_rec[:, 0:512],
                                         ps_all[:, s0 + 512:s0 + 1024]))
                C.w(); C.op(v.tensor_mul(
                    ctxT[g][:, qb * 512:(qb + 1) * 512],
                    ps_all[:, s0:s0 + 512], rn_rec[:, 0:512]), v_cx)
            v.wait_ge(t_st.h, 16)
            C.w(); C.op(v.tensor_scalar_mul(stt_f[:, 0:1024],
                                            ps_all[:, 0:1024], 1.0 / D))
            C.w(); C.op(v.tensor_scalar(stt_f[:, 1024:2048],
                                        ps_all[:, 1024:2048], 1.0 / D,
                                        LN_EPS, ALU.mult, ALU.add))
            C.w(); C.op(v.tensor_mul(stt_f[:, 2048:3072], stt_f[:, 0:1024],
                                     stt_f[:, 0:1024]))
            C.w(); C.op(v.tensor_sub(stt_f[:, 1024:2048], stt_f[:, 1024:2048],
                                     stt_f[:, 2048:3072]), v_ffs)
            v.wait_ge(s_ffq.h, 17)
            C.w(); C.op(v.reciprocal(bc_b[:, 0:1024], stt_f[:, 1024:2048]))
            C.w(); C.op(v.tensor_mul(bc_b[:, 1024:2048], stt_f[:, 0:1024],
                                     bc_b[:, 0:1024]))
            C.w(); C.op(v.tensor_scalar_mul(bc_b[:, 1024:2048],
                                            bc_b[:, 1024:2048], -1.0), v_ffs)
            for d in range(16):
                sl = slice(d * 1024, (d + 1) * 1024)
                C.w(); C.op(v.tensor_mul(sqt[d % 2][:, :], xt_ffn[:, sl],
                                         bc_b[:, 0:1024]))
                C.w(); C.op(v.tensor_add(xt_ffn[:, sl], sqt[d % 2][:, :],
                                         bc_b[:, 1024:2048]), v_xf)

        @blk.sync
        def _(sy):
            for g in range(4):
                sy.wait_ge(v_cx.h, NQB * (g + 1))
                c_agi.inc(sy.dma_start(
                    out=ag_in.ap()[g * 128:(g + 1) * 128, :],
                    in_=ctxT[g][:, :]), 16)

        @blk.gpsimd
        def _(gp):
            g_xr.inc(gp.dma_start(
                out=xt_ffn[:, :].rearrange("p (t c) -> p t c", t=16),
                in_=xresT_hbm.rearrange("(t p) c -> p t c", p=128)), 16)
            gp.wait_ge(c_agi.h, 64)
            c_rs.inc(gp.collective_compute(
                "AllGather", ALU.bypass,
                replica_groups=[list(range(NCORE))],
                ins=[ag_in.ap()], outs=[ag_out.ap()]))

    phB.close()
    scQ.close()
    scC.close()


    phD = ExitStack()
    hidden = phD.enter_context(nc.sbuf_tensor("hidden", [128, 64 * 1024], BF16))
    phD1 = ExitStack()
    wb = [[phD1.enter_context(nc.sbuf_tensor(f"wb{i}{j}", [128, 2048], BF16))
           for j in range(2)] for i in range(4)]
    silu_t = [phD1.enter_context(nc.sbuf_tensor(f"silu{i}", [128, 1024], BF16))
              for i in range(2)]
    ps_ff = phD1.enter_context(nc.psum_tensor("ps_ff", [128, 4096], F32))

    with nc.Block("phD1") as blk:
        @blk.gpsimd
        def _(g):
            for ht in range(64):
                if ht >= 4:
                    g.wait_ge(t_ff.h, ht - 3)
                g_w13[ht % 4].inc(g.dma_start(out=wb[ht % 4][0][:, :],
                                              in_=w1_hbm[ht, :, :]), 16)
                g_w13[ht % 4].inc(g.dma_start(out=wb[ht % 4][1][:, :],
                                              in_=w3_hbm[ht, :, :]), 16)

        @blk.tensor
        def _(t):
            t.wait_ge(v_xf.h, 16)
            for ht in range(64):
                s0 = (ht % 2) * 2048
                t.wait_ge(g_w13[ht % 4].h, 32 * (ht // 4 + 1))
                if ht >= 2:
                    t.wait_ge(v_hd.h, ht - 1)
                    t.wait_ge(s_si.h, ht - 1)
                for d in range(16):
                    for w in range(2):
                        wt = wb[ht % 4][w][:, d * 128:(d + 1) * 128]
                        for th in range(2):
                            xs = xt_ffn[:, d * 1024 + th * 512:
                                        d * 1024 + (th + 1) * 512]
                            mm = t.matmul(
                                ps_ff[:, s0 + w * 1024 + th * 512:
                                      s0 + w * 1024 + (th + 1) * 512],
                                wt, xs, start=(d == 0), stop=(d == 15))
                t_ff.inc(mm)

        @blk.scalar
        def _(s):
            for ht in range(64):
                s0 = (ht % 2) * 2048
                s.wait_ge(t_ff.h, ht + 1)
                s_si.inc(s.activation(silu_t[ht % 2][:, :],
                                      ps_ff[:, s0:s0 + 1024], AF.Silu))

        @blk.vector
        def _(v):
            C = Chain(v, vch)
            for ht in range(64):
                s0 = (ht % 2) * 2048
                v.wait_ge(s_si.h, ht + 1)
                C.w(); C.op(v.tensor_mul(hidden[:, ht * 1024:(ht + 1) * 1024],
                                         silu_t[ht % 2][:, :],
                                         ps_ff[:, s0 + 1024:s0 + 2048]), v_hd)

    phD1.close()
    scXF.close()

    phD2 = ExitStack()
    ws = phD2.enter_context(nc.sbuf_tensor("ws", [128, 64 * 128], F32))
    wsb = phD2.enter_context(nc.sbuf_tensor("wsb", [128, 64 * 128], BF16))
    ps_h = phD2.enter_context(nc.psum_tensor("ps_h", [128, 2 * 512], F32))

    hid3 = hidden[:, :].rearrange("p (t c) -> p t c", t=64)
    hpair = hidden[:, :].rearrange("p (b x) -> p b x", b=32)
    H4 = [[1, 1, 1, 1], [1, -1, 1, -1], [1, 1, -1, -1], [1, -1, -1, 1]]

    VT32, VT4 = 9, 1

    with nc.Block("phD2") as blk:
        @blk.vector
        def _(v):
            C = Chain(v, vch)
            for ts in range(8):
                c0, c1 = ts * 128, (ts + 1) * 128
                hv = hid3[:, :, c0:c1]
                ws3 = ws[:, :].rearrange("p (t c) -> p t c", t=64)
                if ts >= 1:
                    v.wait_ge(g_fw.h, 2 * ts)
                C.w(); C.op(v.tensor_add(ws3[:, 0:VT32, :], hv[:, 0:VT32, :],
                                         hv[:, 32:32 + VT32, :]))
                C.w(); C.op(v.tensor_sub(ws3[:, 32:32 + VT32, :],
                                         hv[:, 0:VT32, :],
                                         hv[:, 32:32 + VT32, :]), v_fw)
                v.wait_ge(g_fw.h, 2 * ts + 1)
                for T in (16, 8):
                    wa = ws[:, :].rearrange("p (b x) -> p b x", b=64 // (2 * T))
                    half = T * 128
                    C.w(); C.op(v.tensor_add(wa[:, :, 0:half], wa[:, :, 0:half],
                                             wa[:, :, half:2 * half]))
                    C.w(); C.op(v.scalar_tensor_tensor(
                        wa[:, :, half:2 * half], wa[:, :, half:2 * half],
                        -2.0, wa[:, :, 0:half], ALU.mult, ALU.add), v_fw)
                if ts >= 1:
                    v.wait_ge(t_h.h, 16 * ts)
                wa = ws[:, :].rearrange("p (b x) -> p b x", b=8)
                wo = wsb[:, :].rearrange("p (b x) -> p b x", b=8)
                vx = VT4 * 128
                C.w(); C.op(v.tensor_add(wo[:, :, 0:vx], wa[:, :, 0:vx],
                                         wa[:, :, 512:512 + vx]))
                C.w(); C.op(v.tensor_sub(wo[:, :, 512:512 + vx], wa[:, :, 0:vx],
                                         wa[:, :, 512:512 + vx]), v_fw)

        @blk.gpsimd
        def _(gp):
            G = Chain(gp, gch)
            for ts in range(8):
                c0, c1 = ts * 128, (ts + 1) * 128
                hv = hid3[:, :, c0:c1]
                ws3 = ws[:, :].rearrange("p (t c) -> p t c", t=64)
                if ts >= 1:
                    gp.wait_ge(v_fw.h, 4 * ts)
                G.w(); G.op(gp.tensor_add(ws3[:, VT32:32, :], hv[:, VT32:32, :],
                                          hv[:, 32 + VT32:64, :]))
                G.w(); G.op(gp.tensor_sub(ws3[:, 32 + VT32:64, :],
                                          hv[:, VT32:32, :],
                                          hv[:, 32 + VT32:64, :]), g_fw)
                gp.wait_ge(v_fw.h, 4 * ts + 3)
                if ts >= 1:
                    gp.wait_ge(t_h.h, 16 * ts)
                wa = ws[:, :].rearrange("p (b x) -> p b x", b=8)
                wo = wsb[:, :].rearrange("p (b x) -> p b x", b=8)
                vx = VT4 * 128
                G.w(); G.op(gp.tensor_add(wo[:, :, vx:512], wa[:, :, vx:512],
                                          wa[:, :, 512 + vx:1024]))
                G.w(); G.op(gp.tensor_sub(wo[:, :, 512 + vx:1024],
                                          wa[:, :, vx:512],
                                          wa[:, :, 512 + vx:1024]), g_fw)

        @blk.tensor
        def _(t):
            t.wait_ge(s_wc.h, 32)
            for ts in range(8):
                t.wait_ge(v_fw.h, 4 * (ts + 1))
                t.wait_ge(g_fw.h, 2 * (ts + 1))
                for k in range(16):
                    pb = ps_h[:, (k % 2) * 512:(k % 2) * 512 + 512]
                    if ts * 16 + k >= 2:
                        t.wait_ge(s_h.h, ts * 16 + k - 1)
                    for i in range(4):
                        for j in range(4):
                            lh = h128_sb if H4[i][j] > 0 else nh128_sb
                            mm = t.matmul(pb[:, i * 128:(i + 1) * 128],
                                          lh[:, :],
                                          wsb[:, (4 * k + j) * 128:
                                              (4 * k + j + 1) * 128],
                                          start=(j == 0), stop=(j == 3))
                    t_h.inc(mm)

        @blk.scalar
        def _(s):
            for ts in range(8):
                for k in range(16):
                    s.wait_ge(t_h.h, ts * 16 + k + 1)
                    pb = ps_h[:, (k % 2) * 512:(k % 2) * 512 + 512]
                    out = hid3[:, k * 4:(k + 1) * 4, ts * 128:(ts + 1) * 128]
                    s_h.inc(s.activation(
                        out, pb.rearrange("p (t c) -> p t c", t=4),
                        AF.Copy, scale=FWHT_SCALE))

    phD2.close()

    phD3 = ExitStack()
    ctxg = phD3.enter_context(nc.sbuf_tensor("ctxg", [128, 16 * 1024], BF16))
    w2b = [phD3.enter_context(nc.sbuf_tensor(f"w2b{i}", [128, 4096], BF16))
           for i in range(3)]
    wp_b = [phD3.enter_context(nc.sbuf_tensor(f"wpb{i}", [128, 2048], BF16))
            for i in range(2)]
    xr_t = phD3.enter_context(nc.sbuf_tensor("xrt", [128, 1024], F32))
    ob = [phD3.enter_context(nc.sbuf_tensor(f"ob{i}", [128, 1024], F32))
          for i in range(2)]
    ps_o = phD3.enter_context(nc.psum_tensor("ps_o", [128, 4096], F32))

    agv = ag_out.ap().rearrange("(j two pp) c -> two pp j c", two=2, pp=128)
    nc.sync.wait_ge(c_rs.h, 1)
    for arm in nc.switch_core_id([nc.sync], n=128):
        lc = arm.logical % NCORE
        bb, rr = lc // 4, lc % 4
        c_ctxg.inc(nc.sync.dma_start(
            out=ctxg[:, :].rearrange("p (j c) -> p j c", j=16),
            in_=agv[bb][:, :, 1024 * rr:1024 * (rr + 1)]), 16)
    c_ctxg.n = 16

    with nc.Block("phD3") as blk:
        @blk.gpsimd
        def _(g):
            for dt in range(16):
                for hf in range(2):
                    m = 2 * dt + hf
                    if m >= 3:
                        g.wait_ge(t_w2.h, 3 * ((m - 3) // 2) + ((m - 3) % 2) + 1)
                    g_w2[m % 3].inc(g.dma_start(
                        out=w2b[m % 3][:, :],
                        in_=w2_hbm[dt, :, hf * 4096:(hf + 1) * 4096]), 16)
                if dt >= 2:
                    g.wait_ge(v_out.h, dt - 1)
                g_wp[dt % 2].inc(g.dma_start(out=wp_b[dt % 2][:, :],
                                             in_=wp_hbm[dt, :, :]), 16)

        @blk.tensor
        def _(t):
            t.wait_ge(c_ctxg.h, 16)
            for dt in range(16):
                s0 = (dt % 2) * 2048
                if dt >= 2:
                    t.wait_ge(v_out.h, dt - 1)
                for hf in range(2):
                    m = 2 * dt + hf
                    t.wait_ge(g_w2[m % 3].h, 16 * (m // 3 + 1))
                    for h in range(32):
                        for th in range(2):
                            mm = t.matmul(
                                ps_o[:, s0 + th * 512:s0 + (th + 1) * 512],
                                w2b[m % 3][:, h * 128:(h + 1) * 128],
                                hidden[:, (hf * 32 + h) * 1024 + th * 512:
                                       (hf * 32 + h) * 1024 + (th + 1) * 512],
                                start=(hf == 0 and h == 0), stop=False)
                    t_w2.inc(mm)
                t.wait_ge(g_wp[dt % 2].h, 16 * (dt // 2 + 1))
                for j in range(16):
                    for th in range(2):
                        mm = t.matmul(
                            ps_o[:, s0 + th * 512:s0 + (th + 1) * 512],
                            wp_b[dt % 2][:, j * 128:(j + 1) * 128],
                            ctxg[:, j * 1024 + th * 512:
                                 j * 1024 + (th + 1) * 512],
                            start=False, stop=(j == 15))
                t_w2.inc(mm)

        @blk.vector
        def _(v):
            C = Chain(v, vch)
            v.wait_ge(s_wc.h, 32)
            for dt in range(16):
                s0 = (dt % 2) * 2048
                v.wait_ge(t_w2.h, 3 * (dt + 1))
                v.wait_ge(s_rs2.h, 16 * (dt + 1))
                if dt >= 2:
                    v.wait_ge(s_out[dt % 2].h, 16 * (dt // 2))
                C.w(); C.op(v.scalar_tensor_tensor(
                    ob[dt % 2][:, :], ps_o[:, s0:s0 + 1024],
                    bprojT_sb[:, dt:dt + 1], xr_t[:, :],
                    ALU.add, ALU.add), v_out)

        @blk.sync
        def _(sy):
            for dt in range(16):
                if dt >= 1:
                    sy.wait_ge(v_out.h, dt)
                s_rs2.inc(sy.dma_start(
                    out=xr_t[:, :],
                    in_=xresT_hbm[dt * 128:(dt + 1) * 128, :]), 16)
                if dt >= 1:
                    s_out[(dt - 1) % 2].inc(sy.dma_start(
                        out=outT_hbm[(dt - 1) * 128:dt * 128, :],
                        in_=ob[(dt - 1) % 2][:, :]), 16)
            sy.wait_ge(v_out.h, 16)
            s_out[1].inc(sy.dma_start(out=outT_hbm[15 * 128:16 * 128, :],
                                      in_=ob[15 % 2][:, :]), 16)
            sy.wait_ge(s_out[0].h, 16 * 8)
            sy.wait_ge(s_out[1].h, 16 * 8)

    phD3.close()
    phD.close()
    ctx.close()
    return nc


def _prep_inputs(L0_attn_query, L0_attn_Wqkv, L0_attn_Wproj, L0_attn_bproj,
                 L0_ffn_W1, L0_ffn_W2, L0_ffn_W3):
    x = np.ascontiguousarray(np.asarray(L0_attn_query, dtype=np.float32))
    Wqkv = np.asarray(L0_attn_Wqkv, dtype=np.float32)
    Wproj = np.asarray(L0_attn_Wproj, dtype=np.float32)
    bproj = np.asarray(L0_attn_bproj, dtype=np.float32)
    W1 = np.asarray(L0_ffn_W1, dtype=np.float32)
    W2 = np.asarray(L0_ffn_W2, dtype=np.float32)
    W3 = np.asarray(L0_ffn_W3, dtype=np.float32)

    WqkvT = np.ascontiguousarray(Wqkv.T)

    def tile_w13(W):
        WT = W.T.reshape(16, 128, 64, 128)
        return np.ascontiguousarray(
            WT.transpose(2, 1, 0, 3).reshape(64, 128, 2048))

    w1_t = tile_w13(W1)
    w3_t = tile_w13(W3)
    W2T = W2.T.reshape(64, 128, 16, 128)
    w2_t = np.ascontiguousarray(W2T.transpose(2, 1, 0, 3).reshape(16, 128, 8192))
    bprojT = np.ascontiguousarray(bproj.reshape(16, 128).T)
    h128 = _hadamard(128).astype(ml_dtypes.bfloat16)
    WpT = Wproj.T.reshape(16, 128, 16, 128)
    _WP_CACHE["wp"] = np.ascontiguousarray(
        WpT.transpose(2, 1, 0, 3).reshape(16, 128, 2048))

    in_maps = []
    for c in range(NCORE):
        b, r = c // 4, c % 4
        heads_local = [4 * r + j for j in range(4)]
        q_cols = [hh * 384 + d for hh in heads_local for d in range(128)]
        k_cols = [hh * 384 + 128 + d for hh in heads_local for d in range(128)]
        v_cols = [hh * 384 + 256 + d for hh in heads_local for d in range(128)]
        qk_cols = np.array(q_cols + k_cols)
        wqk_sel = WqkvT[:, qk_cols].reshape(16, 128, 8, 128)
        wqk_t = np.ascontiguousarray(
            wqk_sel.transpose(2, 1, 0, 3).reshape(8, 128, 2048))
        wv_sel = WqkvT[:, np.array(v_cols)].reshape(16, 128, 512)
        wv_t = np.ascontiguousarray(
            wv_sel.transpose(1, 0, 2).reshape(128, 16 * 512))
        wp_t = _WP_CACHE["wp"]
        x_b = np.ascontiguousarray(x[b])
        x_resT = np.ascontiguousarray(x[b, 1024 * r:1024 * (r + 1), :].T)
        in_maps.append({
            "x_b": x_b, "x_resT": x_resT, "wqk": wqk_t, "wv": wv_t,
            "wp": wp_t, "bprojT": bprojT, "w1": w1_t, "w3": w3_t,
            "w2": w2_t, "h128": h128,
        })
    return in_maps


_NC_CACHE = {}
_WP_CACHE = {}


def kernel(L0_attn_query, L0_attn_key, L0_attn_value, L0_attn_Wqkv,
           L0_attn_Wproj, L0_attn_bproj, L0_ffn_W1, L0_ffn_W2, L0_ffn_W3):
    in_maps = _prep_inputs(L0_attn_query, L0_attn_Wqkv, L0_attn_Wproj,
                           L0_attn_bproj, L0_ffn_W1, L0_ffn_W2, L0_ffn_W3)
    if "nc" not in _NC_CACHE:
        _NC_CACHE["nc"] = build_program()
    nc = _NC_CACHE["nc"]
    res = run_bass_kernel_spmd(nc, in_maps, core_ids=list(range(NCORE)))
    out = np.empty((B, SQ, D), dtype=np.float32)
    for c in range(NCORE):
        b, r = c // 4, c % 4
        out[b, 1024 * r:1024 * (r + 1), :] = res.results[c]["outT"].T
    return out


# version 11
# speedup vs baseline: 1.0540x; 1.0540x over previous
import math
import numpy as np
import ml_dtypes
import concourse.bass as bass
import concourse.mybir as mybir
from concourse.bass_utils import run_bass_kernel_spmd

F32 = mybir.dt.float32
BF16 = mybir.dt.bfloat16
AF = mybir.ActivationFunctionType
ALU = mybir.AluOpType

B, SQ, D = 2, 4096, 2048
NH, HN = 16, 128
H = 8192
NCORE = 8
TOK = 1024
NORM = 1.0 / math.sqrt(HN)
FWHT_SCALE = 1.0 / math.sqrt(H)
LN_EPS = 1e-5
X = mybir.AxisListType.X


class Sem:
    def __init__(self, ctx, nc, name):
        self.h = ctx.enter_context(nc.semaphore(name))
        self.n = 0

    def inc(self, instr, k=1):
        instr.then_inc(self.h, k)
        self.n += k
        return self.n


class Chain:
    def __init__(self, eng, chain_sem):
        self.e = eng
        self.cs = chain_sem
        self.last = None

    def w(self):
        if self.last is not None:
            self.e.wait_ge(*self.last)

    def op(self, instr, sem=None, k=1):
        s = sem if sem is not None else self.cs
        n = s.inc(instr, k)
        self.last = (s.h, n)
        return instr


def _hadamard(n):
    h = np.array([[1.0]], dtype=np.float32)
    while h.shape[0] < n:
        h = np.block([[h, h], [h, -h]])
    return h


def build_program():
    from contextlib import ExitStack
    nc = bass.Bass("TRN2", target_bir_lowering=False,
                   detect_race_conditions=False)
    ctx = ExitStack()

    x_hbm = nc.dram_tensor("x_b", [SQ, D], F32, kind="ExternalInput").ap()
    xresT_hbm = nc.dram_tensor("x_resT", [D, TOK], F32, kind="ExternalInput").ap()
    wqk_hbm = nc.dram_tensor("wqk", [8, 128, 2048], F32, kind="ExternalInput").ap()
    wv_hbm = nc.dram_tensor("wv", [128, 16 * 512], F32, kind="ExternalInput").ap()
    wp_hbm = nc.dram_tensor("wp", [16, 128, 2048], F32, kind="ExternalInput").ap()
    bprojT_hbm = nc.dram_tensor("bprojT", [128, 16], F32, kind="ExternalInput").ap()
    w1_hbm = nc.dram_tensor("w1", [64, 128, 2048], F32, kind="ExternalInput").ap()
    w3_hbm = nc.dram_tensor("w3", [64, 128, 2048], F32, kind="ExternalInput").ap()
    w2_hbm = nc.dram_tensor("w2", [16, 128, 8192], F32, kind="ExternalInput").ap()
    h128_hbm = nc.dram_tensor("h128", [128, 128], BF16, kind="ExternalInput").ap()
    outT_hbm = nc.dram_tensor("outT", [D, TOK], F32, kind="ExternalOutput").ap()

    ag_in = nc.dram_tensor("ag_in", [512, SQ], BF16)
    ag_out = nc.dram_tensor("ag_out", [4096, SQ], BF16, addr_space="Shared")

    sb = ctx.enter_context
    h128_sb = sb(nc.sbuf_tensor("h128_sb", [128, 128], BF16))
    nh128_sb = sb(nc.sbuf_tensor("nh128_sb", [128, 128], BF16))
    ones_mat = sb(nc.sbuf_tensor("ones_mat", [128, 128], BF16))
    bprojT_sb = sb(nc.sbuf_tensor("bprojT_sb", [128, 16], F32))
    negC = sb(nc.sbuf_tensor("negC", [128, 4], F32))
    eps_sb = sb(nc.sbuf_tensor("eps_sb", [128, 1], F32))
    scQ = ExitStack()
    qkT = [scQ.enter_context(nc.sbuf_tensor(f"qkT{i}", [128, SQ], BF16))
           for i in range(8)]
    v_nat = scQ.enter_context(nc.sbuf_tensor("v_nat", [128, 32 * 512], BF16))
    scW = ExitStack()
    wqk_sb = scW.enter_context(nc.sbuf_tensor("wqk_sb", [128, 8 * 2048], BF16))
    wv_sb = scW.enter_context(nc.sbuf_tensor("wv_sb", [128, 16 * 512], BF16))

    g_x = [Sem(ctx, nc, f"g_x{i}") for i in range(3)]
    g_w = Sem(ctx, nc, "g_w")
    s_wc = Sem(ctx, nc, "s_wc")
    v_st = Sem(ctx, nc, "v_st")
    s_sq = Sem(ctx, nc, "s_sq")
    v_ln = Sem(ctx, nc, "v_ln")
    s_tr = [Sem(ctx, nc, f"s_tr{i}") for i in range(3)]
    t_qk = Sem(ctx, nc, "t_qk")
    s_qev = Sem(ctx, nc, "s_qev")
    t_v = Sem(ctx, nc, "t_v")
    s_vev = Sem(ctx, nc, "s_vev")
    s_csq = Sem(ctx, nc, "s_csq")
    t_cg = Sem(ctx, nc, "t_cg")
    v_cg = Sem(ctx, nc, "v_cg")
    s_cg = Sem(ctx, nc, "s_cg")
    v_cgf = Sem(ctx, nc, "v_cgf")
    t_sc = Sem(ctx, nc, "t_sc")
    s_exp = Sem(ctx, nc, "s_exp")
    t_ctx = Sem(ctx, nc, "t_ctx")
    v_cx = Sem(ctx, nc, "v_cx")
    c_agi = Sem(ctx, nc, "c_agi")
    s_rs2 = Sem(ctx, nc, "s_rs2")
    c_ctxg = Sem(ctx, nc, "c_ctxg")
    c_rs = Sem(ctx, nc, "c_rs")
    g_wp = [Sem(ctx, nc, f"g_wp{i}") for i in range(2)]
    g_xr = Sem(ctx, nc, "g_xr")
    t_st = Sem(ctx, nc, "t_st")
    v_ffs = Sem(ctx, nc, "v_ffs")
    s_ffq = Sem(ctx, nc, "s_ffq")
    v_xf = Sem(ctx, nc, "v_xf")
    g_w13 = [Sem(ctx, nc, f"g_w13{i}") for i in range(4)]
    t_ff = Sem(ctx, nc, "t_ff")
    s_si = Sem(ctx, nc, "s_si")
    v_hd = Sem(ctx, nc, "v_hd")
    v_fw = Sem(ctx, nc, "v_fw")
    t_h = Sem(ctx, nc, "t_h")
    s_h = Sem(ctx, nc, "s_h")
    g_w2 = [Sem(ctx, nc, f"g_w2{i}") for i in range(3)]
    t_w2 = Sem(ctx, nc, "t_w2")
    v_out = Sem(ctx, nc, "v_out")
    vch = Sem(ctx, nc, "vch")
    s_out = [Sem(ctx, nc, "s_out0"), Sem(ctx, nc, "s_out1")]

    with nc.Block("ph0") as blk:
        @blk.gpsimd
        def _(g):
            g.memset(ones_mat[:, :], 1.0)
            g.memset(eps_sb[:, :], LN_EPS)
            for ct in range(8):
                g_w.inc(g.dma_start(out=wqk_sb[:, ct * 2048:(ct + 1) * 2048],
                                    in_=wqk_hbm[ct, :, :]), 16)
            g_w.inc(g.dma_start(out=wv_sb[:, :], in_=wv_hbm), 16)

        @blk.sync
        def _(sy):
            s_wc.inc(sy.dma_start(out=h128_sb[:, :], in_=h128_hbm), 16)
            s_wc.inc(sy.dma_start(out=bprojT_sb[:, :], in_=bprojT_hbm), 16)

        @blk.vector
        def _(v):
            v.wait_ge(s_wc.h, 32)
            v.tensor_scalar_mul(nh128_sb[:, :], h128_sb[:, :], -1.0)

    phA = ExitStack()
    XT = [phA.enter_context(nc.sbuf_tensor(f"XT{i}", [128, 16 * 512], BF16))
          for i in range(2)]
    x_nat = [phA.enter_context(nc.sbuf_tensor(f"x_nat{i}", [128, D], BF16))
             for i in range(3)]
    xn_t = [phA.enter_context(nc.sbuf_tensor(f"xn{i}", [128, D], BF16))
            for i in range(3)]
    stats = phA.enter_context(nc.sbuf_tensor("stats", [128, 64], F32))
    ps_qk = phA.enter_context(nc.psum_tensor("ps_qk", [128, 2 * 512], F32))
    ps_qk2 = phA.enter_context(nc.psum_tensor("ps_qk2", [128, 2 * 512], F32))
    ps_v = phA.enter_context(nc.psum_tensor("ps_v", [128, 2 * 512], F32))

    NT, NBLK = 32, 8

    def st(par, col):
        return stats[:, par * 32 + col:par * 32 + col + 1]

    with nc.Block("phA") as blk:
        @blk.gpsimd
        def _(g):
            for gt in range(NT):
                if gt >= 3:
                    g.wait_ge(v_ln.h, gt - 2)
                g_x[gt % 3].inc(g.dma_start(
                    out=x_nat[gt % 3][:, :],
                    in_=x_hbm[gt * 128:(gt + 1) * 128, :]), 16)

        @blk.vector
        def _(v):
            C = Chain(v, vch)
            for gt in range(NT):
                par = gt % 3
                v.wait_ge(g_x[par].h, 16 * (gt // 3 + 1))
                if gt >= 3:
                    v.wait_ge(s_tr[par].h, 16 * (gt // 3))
                xv, xn = x_nat[par], xn_t[par]
                b0 = (gt % 2) * 32
                C.w()
                for q4 in range(4):
                    ins = v.bn_stats(stats[:, b0 + q4 * 6:b0 + (q4 + 1) * 6],
                                     xv[:, q4 * 512:(q4 + 1) * 512])
                C.op(ins)
                C.w(); C.op(v.bn_aggr(stats[:, b0 + 24:b0 + 26],
                                      stats[:, b0:b0 + 24]), v_st)
                v.wait_ge(s_sq.h, gt + 1)
                sp = gt % 2
                C.w(); C.op(v.reciprocal(st(sp, 27), st(sp, 26)))
                C.w(); C.op(v.tensor_scalar(xn[:, :], xv[:, :], st(sp, 24),
                                            st(sp, 27), ALU.subtract, ALU.mult),
                            v_ln)

        @blk.sync
        def _(sy):
            for gt in range(NT):
                bi, t_loc = gt // 4, gt % 4
                if t_loc == 0 and bi >= 2:
                    sy.wait_ge(t_v.h, 4 * (bi - 1))
                sy.wait_ge(v_ln.h, gt + 1)
                xtv = XT[bi % 2][:, :].rearrange("p (j c) -> p j c", j=16)
                s_tr[gt % 3].inc(sy.dma_start_transpose(
                    xtv[:, :, t_loc * 128:(t_loc + 1) * 128],
                    xn_t[gt % 3][:, :].rearrange("p (j c) -> p j c", j=16)), 16)

        @blk.tensor
        def _(t):
            t.wait_ge(g_w.h, g_w.n)
            for bi in range(NBLK):
                xt = XT[bi % 2]
                for pp in range(3):
                    cnt = sum(1 for gg in range(4 * (bi + 1)) if gg % 3 == pp)
                    t.wait_ge(s_tr[pp].h, 16 * cnt)
                for ct in range(8):
                    ps = ps_qk if ct % 2 == 0 else ps_qk2
                    idx = bi * 8 + ct
                    if idx >= 2:
                        t.wait_ge(s_qev.h, idx - 1)
                    for d in range(16):
                        mm = t.matmul(ps[:, 0:512],
                                      wqk_sb[:, ct * 2048 + d * 128:
                                             ct * 2048 + (d + 1) * 128],
                                      xt[:, d * 512:(d + 1) * 512],
                                      start=(d == 0), stop=(d == 15))
                    t_qk.inc(mm)
                for i in range(4):
                    half = i % 2
                    idx = bi * 4 + i
                    if idx >= 2:
                        t.wait_ge(s_vev.h, idx - 1)
                    for d in range(16):
                        mm = t.matmul(ps_v[:, half * 512:(half + 1) * 512],
                                      xt[:, d * 512 + i * 128:
                                         d * 512 + (i + 1) * 128],
                                      wv_sb[:, d * 512:(d + 1) * 512],
                                      start=(d == 0), stop=(d == 15))
                    t_v.inc(mm)

        @blk.scalar
        def _(s):
            for bi in range(NBLK + 1):
                if bi < NBLK:
                    for t_loc in range(4):
                        gt = bi * 4 + t_loc
                        sp = gt % 2
                        s.wait_ge(v_st.h, gt + 1)
                        s_sq.inc(s.activation(st(sp, 26), st(sp, 25),
                                              AF.Sqrt, bias=eps_sb[:, 0:1]))
                if bi >= 1:
                    bj = bi - 1
                    for ct in range(8):
                        ps = ps_qk if ct % 2 == 0 else ps_qk2
                        s.wait_ge(t_qk.h, bj * 8 + ct + 1)
                        s_qev.inc(s.copy(
                            qkT[ct][:, bj * 512:(bj + 1) * 512], ps[:, 0:512]))
                    for i in range(4):
                        half = i % 2
                        s.wait_ge(t_v.h, bj * 4 + i + 1)
                        s_vev.inc(s.copy(
                            v_nat[:, (bj * 4 + i) * 512:(bj * 4 + i + 1) * 512],
                            ps_v[:, half * 512:(half + 1) * 512]))

    phA.close()
    scW.close()

    scXF = ExitStack()
    xt_ffn = scXF.enter_context(nc.sbuf_tensor("xt_ffn", [128, 16 * 1024],
                                               BF16, side="right"))
    scC = ExitStack()
    ctxT = [scC.enter_context(nc.sbuf_tensor(f"ctxT{g}", [128, SQ], BF16,
                                             side="right"))
            for g in range(4)]
    phB = ExitStack()
    sq_sc = phB.enter_context(nc.sbuf_tensor("sq_sc", [128, SQ], BF16))
    cg_tmp = phB.enter_context(nc.sbuf_tensor("cg_tmp", [128, 4], F32))
    E = [phB.enter_context(nc.sbuf_tensor(f"E{i}", [128, 512], BF16))
         for i in range(3)]
    rn_rec = phB.enter_context(nc.sbuf_tensor("rn_rec", [128, 1024], F32))
    sqt = [phB.enter_context(nc.sbuf_tensor(f"sqt{i}", [128, 1024], BF16))
           for i in range(2)]
    stt_f = phB.enter_context(nc.sbuf_tensor("stt_f", [128, 3 * 1024], F32))
    bc_b = phB.enter_context(nc.sbuf_tensor("bc_b", [128, 2 * 1024], F32))
    ps_all = phB.enter_context(nc.psum_tensor("ps_all", [128, 4096], F32))

    with nc.Block("phB_cg") as blk:
        @blk.gpsimd
        def _(gp):
            gp_xr = gp.dma_start(
                out=xt_ffn[:, :].rearrange("p (t c) -> p t c", t=16),
                in_=xresT_hbm.rearrange("(t p) c -> p t c", p=128))
            g_xr.inc(gp_xr, 16)

        @blk.scalar
        def _(s):
            for g in range(4):
                for which in range(2):
                    n = g * 2 + which
                    src = qkT[g] if which == 0 else qkT[4 + g]
                    if n >= 1:
                        s.wait_ge(t_cg.h, 8 * n)
                    s_csq.inc(s.activation(sq_sc[:, :], src[:, :], AF.Square))
                s.wait_ge(v_cg.h, (g + 1) * 16)
                s_cg.inc(s.activation(cg_tmp[:, 2:3], cg_tmp[:, 0:1], AF.Sqrt))
                s_cg.inc(s.activation(cg_tmp[:, 3:4], cg_tmp[:, 1:2], AF.Sqrt))

        @blk.tensor
        def _(t):
            cnt = 0
            for g in range(4):
                for which in range(2):
                    t.wait_ge(s_csq.h, g * 2 + which + 1)
                    for ch in range(8):
                        cnt += 1
                        if cnt >= 3:
                            t.wait_ge(v_cg.h, cnt - 2)
                        t_cg.inc(t.matmul(
                            ps_all[:, (cnt % 2) * 512:(cnt % 2) * 512 + 512],
                            ones_mat[:, :],
                            sq_sc[:, ch * 512:(ch + 1) * 512],
                            start=True, stop=True))

        @blk.vector
        def _(v):
            C = Chain(v, vch)
            cnt = 0
            for g in range(4):
                for which in range(2):
                    col = cg_tmp[:, which:which + 1]
                    for ch in range(8):
                        cnt += 1
                        v.wait_ge(t_cg.h, cnt)
                        pb = ps_all[:, (cnt % 2) * 512:(cnt % 2) * 512 + 512]
                        if ch == 0:
                            C.w(); C.op(v.reduce_max(col, pb, axis=X), v_cg)
                        else:
                            C.w(); C.op(v.reduce_max(rn_rec[:, 0:1], pb, axis=X))
                            C.w(); C.op(v.tensor_max(col, col, rn_rec[:, 0:1]),
                                        v_cg)
                v.wait_ge(s_cg.h, (g + 1) * 2)
                C.w(); C.op(v.tensor_mul(negC[:, g:g + 1], cg_tmp[:, 2:3],
                                         cg_tmp[:, 3:4]))
                C.w(); C.op(v.tensor_scalar_mul(negC[:, g:g + 1],
                                                negC[:, g:g + 1], -NORM), v_cgf)

    with nc.Block("phB") as blk:
        NQB = 8
        @blk.tensor
        def _(t):
            t.wait_ge(g_xr.h, 16)
            for d in range(16):
                t.wait_ge(s_ffq.h, d + 1)
                for th in range(2):
                    t.matmul(ps_all[:, th * 512:(th + 1) * 512], ones_mat[:, :],
                             xt_ffn[:, d * 1024 + th * 512:
                                    d * 1024 + (th + 1) * 512],
                             start=(d == 0), stop=(d == 15))
                    mm = t.matmul(ps_all[:, 1024 + th * 512:1024 + (th + 1) * 512],
                                  ones_mat[:, :],
                                  sqt[d % 2][:, th * 512:(th + 1) * 512],
                                  start=(d == 0), stop=(d == 15))
                t_st.inc(mm)
            t.wait_ge(v_cgf.h, 4)
            t.wait_ge(v_ffs.h, 1)
            steps = [(g, qb, kt) for g in range(4) for qb in range(NQB)
                     for kt in range(32)]
            nsteps = len(steps)

            def scores(n):
                g, qb, kt = steps[n]
                sc = ps_all[:, (n % 3) * 512:(n % 3) * 512 + 512]
                if n >= 3:
                    t.wait_ge(s_exp.h, n - 2)
                t_sc.inc(t.matmul(sc, qkT[4 + g][:, kt * 128:(kt + 1) * 128],
                                  qkT[g][:, qb * 512:(qb + 1) * 512],
                                  start=True, stop=True))

            scores(0)
            for n in range(nsteps):
                g, qb, kt = steps[n]
                qf = g * NQB + qb
                s0 = 2048 + (qf % 2) * 1024
                if n + 1 < nsteps:
                    scores(n + 1)
                t.wait_ge(s_exp.h, n + 1)
                if kt == 0 and qf >= 2:
                    t.wait_ge(v_cx.h, qf - 1)
                eb = E[n % 3]
                vsl = v_nat[:, kt * 512 + g * 128:kt * 512 + (g + 1) * 128]
                t.matmul(ps_all[:, s0:s0 + 512], vsl, eb[:, :],
                         start=(kt == 0), stop=(kt == 31))
                mm = t.matmul(ps_all[:, s0 + 512:s0 + 1024], ones_mat[:, :],
                              eb[:, :], start=(kt == 0), stop=(kt == 31))
                if kt == 31:
                    t_ctx.inc(mm)

        @blk.scalar
        def _(s):
            s.wait_ge(g_xr.h, 16)
            for d in range(16):
                if d >= 2:
                    s.wait_ge(t_st.h, d - 1)
                s_ffq.inc(s.activation(sqt[d % 2][:, :],
                                       xt_ffn[:, d * 1024:(d + 1) * 1024],
                                       AF.Square))
            s.wait_ge(v_ffs.h, 1)
            s_ffq.inc(s.activation(stt_f[:, 1024:2048], stt_f[:, 1024:2048],
                                   AF.Sqrt))
            s.wait_ge(v_cgf.h, 4)
            n = 0
            for g in range(4):
                for qb in range(NQB):
                    for kt in range(32):
                        n += 1
                        s.wait_ge(t_sc.h, n)
                        sc = ps_all[:, ((n - 1) % 3) * 512:((n - 1) % 3) * 512 + 512]
                        s_exp.inc(s.activation(E[(n - 1) % 3][:, :], sc,
                                               AF.Exp, bias=negC[:, g:g + 1],
                                               scale=NORM))

        @blk.vector
        def _(v):
            C = Chain(v, vch)
            v.wait_ge(t_st.h, 16)
            C.w(); C.op(v.tensor_scalar_mul(stt_f[:, 0:1024],
                                            ps_all[:, 0:1024], 1.0 / D))
            C.w(); C.op(v.tensor_scalar(stt_f[:, 1024:2048],
                                        ps_all[:, 1024:2048], 1.0 / D,
                                        LN_EPS, ALU.mult, ALU.add))
            C.w(); C.op(v.tensor_mul(stt_f[:, 2048:3072], stt_f[:, 0:1024],
                                     stt_f[:, 0:1024]))
            C.w(); C.op(v.tensor_sub(stt_f[:, 1024:2048], stt_f[:, 1024:2048],
                                     stt_f[:, 2048:3072]), v_ffs)
            v.wait_ge(s_ffq.h, 17)
            C.w(); C.op(v.reciprocal(bc_b[:, 0:1024], stt_f[:, 1024:2048]))
            C.w(); C.op(v.tensor_mul(bc_b[:, 1024:2048], stt_f[:, 0:1024],
                                     bc_b[:, 0:1024]))
            C.w(); C.op(v.tensor_scalar_mul(bc_b[:, 1024:2048],
                                            bc_b[:, 1024:2048], -1.0), v_ffs)
            for d in range(16):
                sl = slice(d * 1024, (d + 1) * 1024)
                C.w(); C.op(v.tensor_mul(sqt[d % 2][:, :], xt_ffn[:, sl],
                                         bc_b[:, 0:1024]))
                C.w(); C.op(v.tensor_add(xt_ffn[:, sl], sqt[d % 2][:, :],
                                         bc_b[:, 1024:2048]), v_xf)
            for qf in range(4 * NQB):
                g, qb = qf // NQB, qf % NQB
                s0 = 2048 + (qf % 2) * 1024
                v.wait_ge(t_ctx.h, qf + 1)
                C.w(); C.op(v.reciprocal(rn_rec[:, 0:512],
                                         ps_all[:, s0 + 512:s0 + 1024]))
                C.w(); C.op(v.tensor_mul(
                    ctxT[g][:, qb * 512:(qb + 1) * 512],
                    ps_all[:, s0:s0 + 512], rn_rec[:, 0:512]), v_cx)

        @blk.sync
        def _(sy):
            for g in range(4):
                sy.wait_ge(v_cx.h, NQB * (g + 1))
                c_agi.inc(sy.dma_start(
                    out=ag_in.ap()[g * 128:(g + 1) * 128, :],
                    in_=ctxT[g][:, :]), 16)

        @blk.gpsimd
        def _(gp):
            gp.wait_ge(c_agi.h, 64)
            c_rs.inc(gp.collective_compute(
                "AllGather", ALU.bypass,
                replica_groups=[list(range(NCORE))],
                ins=[ag_in.ap()], outs=[ag_out.ap()]))

    phB.close()
    scQ.close()
    scC.close()


    phD = ExitStack()
    hidden = phD.enter_context(nc.sbuf_tensor("hidden", [128, 64 * 1024], BF16))
    phD1 = ExitStack()
    wb = [[phD1.enter_context(nc.sbuf_tensor(f"wb{i}{j}", [128, 2048], BF16))
           for j in range(2)] for i in range(4)]
    silu_t = [phD1.enter_context(nc.sbuf_tensor(f"silu{i}", [128, 1024], BF16))
              for i in range(2)]
    ps_ff = phD1.enter_context(nc.psum_tensor("ps_ff", [128, 4096], F32))

    with nc.Block("phD1") as blk:
        @blk.gpsimd
        def _(g):
            for ht in range(64):
                if ht >= 4:
                    g.wait_ge(t_ff.h, ht - 3)
                g_w13[ht % 4].inc(g.dma_start(out=wb[ht % 4][0][:, :],
                                              in_=w1_hbm[ht, :, :]), 16)
                g_w13[ht % 4].inc(g.dma_start(out=wb[ht % 4][1][:, :],
                                              in_=w3_hbm[ht, :, :]), 16)

        @blk.tensor
        def _(t):
            t.wait_ge(v_xf.h, 16)
            for ht in range(64):
                s0 = (ht % 2) * 2048
                t.wait_ge(g_w13[ht % 4].h, 32 * (ht // 4 + 1))
                if ht >= 2:
                    t.wait_ge(v_hd.h, ht - 1)
                    t.wait_ge(s_si.h, ht - 1)
                for d in range(16):
                    for w in range(2):
                        wt = wb[ht % 4][w][:, d * 128:(d + 1) * 128]
                        for th in range(2):
                            xs = xt_ffn[:, d * 1024 + th * 512:
                                        d * 1024 + (th + 1) * 512]
                            mm = t.matmul(
                                ps_ff[:, s0 + w * 1024 + th * 512:
                                      s0 + w * 1024 + (th + 1) * 512],
                                wt, xs, start=(d == 0), stop=(d == 15))
                t_ff.inc(mm)

        @blk.scalar
        def _(s):
            for ht in range(64):
                s0 = (ht % 2) * 2048
                s.wait_ge(t_ff.h, ht + 1)
                s_si.inc(s.activation(silu_t[ht % 2][:, :],
                                      ps_ff[:, s0:s0 + 1024], AF.Silu))

        @blk.vector
        def _(v):
            C = Chain(v, vch)
            for ht in range(64):
                s0 = (ht % 2) * 2048
                v.wait_ge(s_si.h, ht + 1)
                C.w(); C.op(v.tensor_mul(hidden[:, ht * 1024:(ht + 1) * 1024],
                                         silu_t[ht % 2][:, :],
                                         ps_ff[:, s0 + 1024:s0 + 2048]), v_hd)

    phD1.close()
    scXF.close()

    phD2 = ExitStack()
    ws = phD2.enter_context(nc.sbuf_tensor("ws", [128, 64 * 128], F32))
    wsb = phD2.enter_context(nc.sbuf_tensor("wsb", [128, 64 * 128], BF16))
    ps_h = phD2.enter_context(nc.psum_tensor("ps_h", [128, 2 * 512], F32))

    hid3 = hidden[:, :].rearrange("p (t c) -> p t c", t=64)
    hpair = hidden[:, :].rearrange("p (b x) -> p b x", b=32)
    H4 = [[1, 1, 1, 1], [1, -1, 1, -1], [1, 1, -1, -1], [1, -1, -1, 1]]

    with nc.Block("phD2") as blk:
        @blk.vector
        def _(v):
            C = Chain(v, vch)
            for ts in range(8):
                c0, c1 = ts * 128, (ts + 1) * 128
                hv = hid3[:, :, c0:c1]
                ws3 = ws[:, :].rearrange("p (t c) -> p t c", t=64)
                C.w(); C.op(v.tensor_add(ws3[:, 0:32, :], hv[:, 0:32, :],
                                         hv[:, 32:64, :]))
                C.w(); C.op(v.tensor_sub(ws3[:, 32:64, :], hv[:, 0:32, :],
                                         hv[:, 32:64, :]))
                for T in (16, 8):
                    wa = ws[:, :].rearrange("p (b x) -> p b x", b=64 // (2 * T))
                    half = T * 128
                    C.w(); C.op(v.tensor_add(wa[:, :, 0:half], wa[:, :, 0:half],
                                             wa[:, :, half:2 * half]))
                    C.w(); C.op(v.scalar_tensor_tensor(
                        wa[:, :, half:2 * half], wa[:, :, half:2 * half], -2.0,
                        wa[:, :, 0:half], ALU.mult, ALU.add))
                if ts >= 1:
                    v.wait_ge(t_h.h, 16 * ts)
                wa = ws[:, :].rearrange("p (b x) -> p b x", b=8)
                wo = wsb[:, :].rearrange("p (b x) -> p b x", b=8)
                C.w(); C.op(v.tensor_add(wo[:, :, 0:512], wa[:, :, 0:512],
                                         wa[:, :, 512:1024]))
                C.w(); C.op(v.tensor_sub(wo[:, :, 512:1024], wa[:, :, 0:512],
                                         wa[:, :, 512:1024]), v_fw)

        @blk.tensor
        def _(t):
            t.wait_ge(s_wc.h, 32)
            for ts in range(8):
                t.wait_ge(v_fw.h, ts + 1)
                for k in range(16):
                    pb = ps_h[:, (k % 2) * 512:(k % 2) * 512 + 512]
                    if ts * 16 + k >= 2:
                        t.wait_ge(s_h.h, ts * 16 + k - 1)
                    for i in range(4):
                        for j in range(4):
                            lh = h128_sb if H4[i][j] > 0 else nh128_sb
                            mm = t.matmul(pb[:, i * 128:(i + 1) * 128],
                                          lh[:, :],
                                          wsb[:, (4 * k + j) * 128:
                                              (4 * k + j + 1) * 128],
                                          start=(j == 0), stop=(j == 3))
                    t_h.inc(mm)

        @blk.scalar
        def _(s):
            for ts in range(8):
                for k in range(16):
                    s.wait_ge(t_h.h, ts * 16 + k + 1)
                    pb = ps_h[:, (k % 2) * 512:(k % 2) * 512 + 512]
                    out = hid3[:, k * 4:(k + 1) * 4, ts * 128:(ts + 1) * 128]
                    s_h.inc(s.activation(
                        out, pb.rearrange("p (t c) -> p t c", t=4),
                        AF.Copy, scale=FWHT_SCALE))

    phD2.close()

    phD3 = ExitStack()
    ctxg = phD3.enter_context(nc.sbuf_tensor("ctxg", [128, 16 * 1024], BF16))
    w2b = [phD3.enter_context(nc.sbuf_tensor(f"w2b{i}", [128, 4096], BF16))
           for i in range(3)]
    wp_b = [phD3.enter_context(nc.sbuf_tensor(f"wpb{i}", [128, 2048], BF16))
            for i in range(2)]
    xr_t = phD3.enter_context(nc.sbuf_tensor("xrt", [128, 1024], F32))
    ob = [phD3.enter_context(nc.sbuf_tensor(f"ob{i}", [128, 1024], F32))
          for i in range(2)]
    ps_o = phD3.enter_context(nc.psum_tensor("ps_o", [128, 4096], F32))

    agv = ag_out.ap().rearrange("(j two pp) c -> two pp j c", two=2, pp=128)
    nc.sync.wait_ge(c_rs.h, 1)
    for arm in nc.switch_core_id([nc.sync], n=128):
        lc = arm.logical % NCORE
        bb, rr = lc // 4, lc % 4
        c_ctxg.inc(nc.sync.dma_start(
            out=ctxg[:, :].rearrange("p (j c) -> p j c", j=16),
            in_=agv[bb][:, :, 1024 * rr:1024 * (rr + 1)]), 16)
    c_ctxg.n = 16

    with nc.Block("phD3") as blk:
        @blk.gpsimd
        def _(g):
            for dt in range(16):
                for hf in range(2):
                    m = 2 * dt + hf
                    if m >= 3:
                        g.wait_ge(t_w2.h, 3 * ((m - 3) // 2) + ((m - 3) % 2) + 1)
                    g_w2[m % 3].inc(g.dma_start(
                        out=w2b[m % 3][:, :],
                        in_=w2_hbm[dt, :, hf * 4096:(hf + 1) * 4096]), 16)
                if dt >= 2:
                    g.wait_ge(v_out.h, dt - 1)
                g_wp[dt % 2].inc(g.dma_start(out=wp_b[dt % 2][:, :],
                                             in_=wp_hbm[dt, :, :]), 16)

        @blk.tensor
        def _(t):
            t.wait_ge(c_ctxg.h, 16)
            for dt in range(16):
                s0 = (dt % 2) * 2048
                if dt >= 2:
                    t.wait_ge(v_out.h, dt - 1)
                for hf in range(2):
                    m = 2 * dt + hf
                    t.wait_ge(g_w2[m % 3].h, 16 * (m // 3 + 1))
                    for h in range(32):
                        for th in range(2):
                            mm = t.matmul(
                                ps_o[:, s0 + th * 512:s0 + (th + 1) * 512],
                                w2b[m % 3][:, h * 128:(h + 1) * 128],
                                hidden[:, (hf * 32 + h) * 1024 + th * 512:
                                       (hf * 32 + h) * 1024 + (th + 1) * 512],
                                start=(hf == 0 and h == 0), stop=False)
                    t_w2.inc(mm)
                t.wait_ge(g_wp[dt % 2].h, 16 * (dt // 2 + 1))
                for j in range(16):
                    for th in range(2):
                        mm = t.matmul(
                            ps_o[:, s0 + th * 512:s0 + (th + 1) * 512],
                            wp_b[dt % 2][:, j * 128:(j + 1) * 128],
                            ctxg[:, j * 1024 + th * 512:
                                 j * 1024 + (th + 1) * 512],
                            start=False, stop=(j == 15))
                t_w2.inc(mm)

        @blk.vector
        def _(v):
            C = Chain(v, vch)
            v.wait_ge(s_wc.h, 32)
            for dt in range(16):
                s0 = (dt % 2) * 2048
                v.wait_ge(t_w2.h, 3 * (dt + 1))
                v.wait_ge(s_rs2.h, 16 * (dt + 1))
                if dt >= 2:
                    v.wait_ge(s_out[dt % 2].h, 16 * (dt // 2))
                C.w(); C.op(v.scalar_tensor_tensor(
                    ob[dt % 2][:, :], ps_o[:, s0:s0 + 1024],
                    bprojT_sb[:, dt:dt + 1], xr_t[:, :],
                    ALU.add, ALU.add), v_out)

        @blk.sync
        def _(sy):
            for dt in range(16):
                if dt >= 1:
                    sy.wait_ge(v_out.h, dt)
                s_rs2.inc(sy.dma_start(
                    out=xr_t[:, :],
                    in_=xresT_hbm[dt * 128:(dt + 1) * 128, :]), 16)
                if dt >= 1:
                    s_out[(dt - 1) % 2].inc(sy.dma_start(
                        out=outT_hbm[(dt - 1) * 128:dt * 128, :],
                        in_=ob[(dt - 1) % 2][:, :]), 16)
            sy.wait_ge(v_out.h, 16)
            s_out[1].inc(sy.dma_start(out=outT_hbm[15 * 128:16 * 128, :],
                                      in_=ob[15 % 2][:, :]), 16)
            sy.wait_ge(s_out[0].h, 16 * 8)
            sy.wait_ge(s_out[1].h, 16 * 8)

    phD3.close()
    phD.close()
    ctx.close()
    return nc


def _prep_inputs(L0_attn_query, L0_attn_Wqkv, L0_attn_Wproj, L0_attn_bproj,
                 L0_ffn_W1, L0_ffn_W2, L0_ffn_W3):
    x = np.ascontiguousarray(np.asarray(L0_attn_query, dtype=np.float32))
    Wqkv = np.asarray(L0_attn_Wqkv, dtype=np.float32)
    Wproj = np.asarray(L0_attn_Wproj, dtype=np.float32)
    bproj = np.asarray(L0_attn_bproj, dtype=np.float32)
    W1 = np.asarray(L0_ffn_W1, dtype=np.float32)
    W2 = np.asarray(L0_ffn_W2, dtype=np.float32)
    W3 = np.asarray(L0_ffn_W3, dtype=np.float32)

    WqkvT = np.ascontiguousarray(Wqkv.T)

    def tile_w13(W):
        WT = W.T.reshape(16, 128, 64, 128)
        return np.ascontiguousarray(
            WT.transpose(2, 1, 0, 3).reshape(64, 128, 2048))

    w1_t = tile_w13(W1)
    w3_t = tile_w13(W3)
    W2T = W2.T.reshape(64, 128, 16, 128)
    w2_t = np.ascontiguousarray(W2T.transpose(2, 1, 0, 3).reshape(16, 128, 8192))
    bprojT = np.ascontiguousarray(bproj.reshape(16, 128).T)
    h128 = _hadamard(128).astype(ml_dtypes.bfloat16)
    WpT = Wproj.T.reshape(16, 128, 16, 128)
    _WP_CACHE["wp"] = np.ascontiguousarray(
        WpT.transpose(2, 1, 0, 3).reshape(16, 128, 2048))

    in_maps = []
    for c in range(NCORE):
        b, r = c // 4, c % 4
        heads_local = [4 * r + j for j in range(4)]
        q_cols = [hh * 384 + d for hh in heads_local for d in range(128)]
        k_cols = [hh * 384 + 128 + d for hh in heads_local for d in range(128)]
        v_cols = [hh * 384 + 256 + d for hh in heads_local for d in range(128)]
        qk_cols = np.array(q_cols + k_cols)
        wqk_sel = WqkvT[:, qk_cols].reshape(16, 128, 8, 128)
        wqk_t = np.ascontiguousarray(
            wqk_sel.transpose(2, 1, 0, 3).reshape(8, 128, 2048))
        wv_sel = WqkvT[:, np.array(v_cols)].reshape(16, 128, 512)
        wv_t = np.ascontiguousarray(
            wv_sel.transpose(1, 0, 2).reshape(128, 16 * 512))
        wp_t = _WP_CACHE["wp"]
        x_b = np.ascontiguousarray(x[b])
        x_resT = np.ascontiguousarray(x[b, 1024 * r:1024 * (r + 1), :].T)
        in_maps.append({
            "x_b": x_b, "x_resT": x_resT, "wqk": wqk_t, "wv": wv_t,
            "wp": wp_t, "bprojT": bprojT, "w1": w1_t, "w3": w3_t,
            "w2": w2_t, "h128": h128,
        })
    return in_maps


_NC_CACHE = {}
_WP_CACHE = {}


def kernel(L0_attn_query, L0_attn_key, L0_attn_value, L0_attn_Wqkv,
           L0_attn_Wproj, L0_attn_bproj, L0_ffn_W1, L0_ffn_W2, L0_ffn_W3):
    in_maps = _prep_inputs(L0_attn_query, L0_attn_Wqkv, L0_attn_Wproj,
                           L0_attn_bproj, L0_ffn_W1, L0_ffn_W2, L0_ffn_W3)
    if "nc" not in _NC_CACHE:
        _NC_CACHE["nc"] = build_program()
    nc = _NC_CACHE["nc"]
    res = run_bass_kernel_spmd(nc, in_maps, core_ids=list(range(NCORE)))
    out = np.empty((B, SQ, D), dtype=np.float32)
    for c in range(NCORE):
        b, r = c // 4, c % 4
        out[b, 1024 * r:1024 * (r + 1), :] = res.results[c]["outT"].T
    return out
